# version 02 (final)
# baseline (speedup 1.0000x reference)
import numpy as np
import ml_dtypes
import concourse.bass as bass
import concourse.mybir as mybir
from concourse.bass_utils import run_bass_kernel_spmd

F32 = mybir.dt.float32
BF16 = mybir.dt.bfloat16
AF = mybir.ActivationFunctionType
ALU = mybir.AluOpType
AX = mybir.AxisListType
BF = ml_dtypes.bfloat16

N_CORES = 8
T = 2048
D = 1024
NT = T // 128
DC = D // 128

ENGS = ("pe", "act", "dve", "pool", "sp")


class KB:

    def __init__(self):
        self.ops = []
        self.cnt = {}
        self.hwm = {e: {} for e in ENGS}

    def emit(self, eng, fn, deps=(), sem=None):
        waits = []
        for t in deps:
            if t is None:
                continue
            s, v = t
            if v > self.hwm[eng].get(s, 0):
                self.hwm[eng][s] = v
                waits.append((s, v))
        if sem is None:
            sem = eng
        tik = None
        if sem is not False:
            amt = 16 if sem.startswith("dma") else 1
            self.cnt[sem] = self.cnt.get(sem, 0) + amt
            tik = (sem, self.cnt[sem])
        self.ops.append((eng, waits, fn, tik))
        return tik

    def dma(self, eng, out, in_, deps=()):
        return self.emit(eng, lambda e: e.dma_start(out=out, in_=in_), deps,
                         sem="dma_" + eng)

    def barrier(self):
        return [(s, v) for s, v in self.cnt.items()]

    def finalize(self, nc):
        sem_names = sorted(self.cnt.keys())
        from contextlib import ExitStack
        with ExitStack() as st:
            sems = {s: st.enter_context(nc.semaphore(s)) for s in sem_names}
            blk = st.enter_context(nc.Block())

            def run_engine(engobj, ename):
                for (eng, waits, fn, tik) in self.ops:
                    if eng != ename:
                        continue
                    for (s, v) in waits:
                        engobj.wait_ge(sems[s], v)
                    inst = fn(engobj)
                    if tik is not None:
                        assert inst is not None, f"op on {ename} has no inst"
                        inst.then_inc(sems[tik[0]],
                                      16 if tik[0].startswith("dma") else 1)

            blk.tensor(lambda e: run_engine(e, "pe"))
            blk.scalar(lambda e: run_engine(e, "act"))
            blk.vector(lambda e: run_engine(e, "dve"))
            blk.gpsimd(lambda e: run_engine(e, "pool"))
            blk.sync(lambda e: run_engine(e, "sp"))
        return nc



def host_identity128():
    return np.eye(128, dtype=np.float32).astype(BF)


def host_rotary_tables():
    DIM_HEAD, WINDOW = 64, 256
    half = np.arange(0, DIM_HEAD, 2, dtype=np.float64)
    inv_freq = 1.0 / 10000.0 ** (half / DIM_HEAD)
    t = np.arange(T, dtype=np.float64)
    freqs = t[:, None] * inv_freq[None, :]
    freqs = np.concatenate([freqs, freqs], -1)
    base_scale = (half + 0.4 * DIM_HEAD) / (1.4 * DIM_HEAD)
    power = (t - T // 2) / (WINDOW // 2)
    scale = base_scale[None, :] ** power[:, None]
    scale = np.concatenate([scale, scale], -1)
    cos, sin = np.cos(freqs), np.sin(freqs)
    inv = 1.0 / scale
    sgn = np.where(np.arange(64) < 32, -1.0, 1.0)[None, :]
    return dict(cs_q=(cos * scale).astype(np.float32),
                sn_q=(sin * scale * sgn).astype(np.float32),
                cs_k=(cos * inv).astype(np.float32),
                sn_k=(sin * inv * sgn).astype(np.float32))


def host_l0f_masks():
    out = []
    for qi in range(NT):
        Qb = qi * 128
        kb_ = max(0, Qb - 256)
        p = np.arange(128)[:, None]
        f = np.arange(384)[None, :]
        i = Qb + p
        j = kb_ + f
        valid = (j <= i) & (j >= i - 256)
        out.append(np.where(valid, 0.0, -1e30).astype(np.float32))
    return np.stack([out[0], out[1], out[2]], axis=0)


def host_causal_diag_mask():
    p = np.arange(128)[:, None]
    f = np.arange(128)[None, :]
    return np.where(f <= p, 0.0, -1e30).astype(np.float32)



def emit_ln_stats(kb, x_ap, sq_ap, st8, deps, eps=1e-5):
    F = x_ap.shape[-1]
    t_sq = kb.emit("act", lambda e: e.activation(sq_ap, x_ap, AF.Square), deps)
    t_s1 = kb.emit("dve", lambda e: e.reduce_sum(st8[:, 0:1], x_ap, AX.X), deps)
    t_s2 = kb.emit("dve", lambda e: e.reduce_sum(st8[:, 1:2], sq_ap, AX.X), [t_sq, t_s1])
    t_mu = kb.emit("dve", lambda e: e.tensor_scalar_mul(st8[:, 2:3], st8[:, 0:1], 1.0 / F), [t_s2])
    t_m2 = kb.emit("dve", lambda e: e.tensor_mul(st8[:, 4:5], st8[:, 2:3], st8[:, 2:3]), [t_mu])
    t_v = kb.emit("dve", lambda e: e.scalar_tensor_tensor(
        st8[:, 0:1], st8[:, 1:2], 1.0 / F, st8[:, 4:5], ALU.mult, ALU.subtract), [t_m2])
    t_sd = kb.emit("act", lambda e: e.activation(st8[:, 1:2], st8[:, 0:1], AF.Sqrt,
                                                 bias=kb.epscol, scale=1.0), [t_v])
    t_r = kb.emit("dve", lambda e: e.reciprocal(st8[:, 3:4], st8[:, 1:2]), [t_sd])
    return t_r


def emit_ln_apply(kb, out_ap, x_ap, st8, deps, eng="dve"):
    return kb.emit(eng, lambda e: e.tensor_scalar(
        out_ap, x_ap, st8[:, 2:3], st8[:, 3:4], ALU.subtract, ALU.mult), deps)


class TransposeHelper:

    def __init__(self, kb, ident, ps, bank_f32_col):
        self.kb = kb
        self.ident = ident
        self.stage = ps[:, bank_f32_col:bank_f32_col + 512].bitcast(BF16)
        self.war = None

    def run(self, src_bf16_tile, dst_view_fn, deps, nblocks, evac_eng="dve",
            evac_deps=()):
        kb = self.kb
        last = None
        for j0 in range(0, nblocks, 8):
            cnt = min(8, nblocks - j0)
            tts = []
            for j in range(cnt):
                src_blk = src_bf16_tile[:, (j0 + j) * 128:(j0 + j + 1) * 128]
                out_blk = self.stage[:, j * 128:(j + 1) * 128]
                dd = list(deps) + ([self.war] if self.war is not None else [])
                tts.append(kb.emit("pe", (lambda e, o=out_blk, s=src_blk:
                                          e.transpose(o, s, self.ident)), dd))
            out_ap, in_ap = dst_view_fn(j0, cnt, self.stage)
            self.war = kb.emit(evac_eng, (lambda e, o=out_ap, i=in_ap:
                                          e.tensor_copy(o, i)),
                               tts + list(evac_deps))
            last = self.war
        return last



def build_l0attn(nc, kb, io):
    SCALE = 1.0 / np.sqrt(D)

    a = nc.alloc_sbuf_tensor("Abf", [128, 81920], BF16).ap()
    f = nc.alloc_sbuf_tensor("Af32", [128, 6144], F32).ap()
    ps = nc.alloc_psum_tensor("PS", [128, 4096], F32).ap()

    XF = lambda dd: a[:, dd * 2048:(dd + 1) * 2048]
    QF = lambda dd: a[:, 16384 + dd * 2048:16384 + (dd + 1) * 2048]
    KF = lambda dd: a[:, 32768 + dd * 2048:32768 + (dd + 1) * 2048]
    VT = lambda ti: a[:, 49152 + ti * 1024:49152 + (ti + 1) * 1024]
    WQK = lambda dd: a[:, 65536 + dd * 2048:65536 + (dd + 1) * 2048]
    WV = lambda dd: a[:, 65536 + dd * 1024:65536 + (dd + 1) * 1024]
    WO = lambda dd: a[:, 65536 + dd * 1024:65536 + (dd + 1) * 1024]
    PBUF = lambda hh: a[:, 73728 + hh * 2048:73728 + (hh + 1) * 2048]
    AT = lambda jp: a[:, 77824 + jp * 512:77824 + (jp + 1) * 512]
    PT = lambda hh: a[:, 0 + hh * 8192:(hh + 1) * 8192]

    XT1 = f[:, 0:1024]
    XT2 = f[:, 1024:2048]
    SQ = f[:, 2048:3072]
    MIX = f[:, 3072:4096]
    ST = lambda i: f[:, 4096 + i * 8:4096 + (i + 1) * 8]
    ST2 = f[:, 4224:4232]
    DEN = f[:, 4232:4236]
    BIA = f[:, 4236:4240]
    XNR = f[:, 4240:5264]
    EPS = f[:, 5264:5265]
    kb.epscol = EPS

    ROWS = nc.alloc_sbuf_tensor("ROWS", [1, 4096], BF16).ap()
    ONE_BF = ROWS[0:1, 0:128]
    BVROW = ROWS[0:1, 128:1152]
    BOROW = ROWS[0:1, 1152:2176]
    IDT = nc.alloc_sbuf_tensor("IDT", [128, 128], BF16).ap()
    BQC = nc.alloc_sbuf_tensor("BQC", [128, 16], F32).ap()

    t_eps = kb.emit("dve", lambda e: e.memset(EPS, 1e-5))
    t_one = kb.emit("dve", lambda e: e.memset(ONE_BF, 1.0))
    t_id = kb.dma("sp", IDT, io["ident"])
    t_bq = kb.dma("sp", BQC, io["bq_cols"])
    t_bv = kb.dma("pool", BVROW, io["bv_row"])
    t_bo = kb.dma("pool", BOROW, io["bo_row"])
    t_wqk = []
    for dd in range(DC):
        t_wqk.append(kb.dma("pool", WQK(dd),
                            io["wqkvT"][dd * 128:(dd + 1) * 128, 0:2048]))

    if "wq0e" in io.get("dbg", {}):
        kb.dma("sp", io["dbg"]["wq0e"], WQK(0), deps=t_wqk)
    th = TransposeHelper(kb, IDT, ps, bank_f32_col=4 * 512)

    t_xf_done = []
    prev_ln = None
    if "xn_fm_direct" in io:
        for dd in range(DC):
            t_xf_done.append(kb.dma("pool", XF(dd),
                                    io["xn_fm_direct"][dd * 128:(dd + 1) * 128, :]))
        prev_ln = t_xf_done[-1]
    for i in range(NT):
        if "xn_fm_direct" in io:
            break
        xt = XT1 if i % 2 == 0 else XT2
        t_x = kb.dma("sp", xt, io["x"][i * 128:(i + 1) * 128, :],
                     deps=[prev_ln] if prev_ln else [])
        t_st = emit_ln_stats(kb, xt, SQ, ST(i), [t_x, t_eps])
        xn_t = QF(0)[:, 0:1024]
        t_ap = emit_ln_apply(kb, xn_t, xt, ST(i), [t_st] + ([th.war] if th.war else []))
        prev_ln = t_ap

        def dst_fn(j0, cnt, stage, i=i):
            outv = a[:, 0:16384].rearrange("p (c t) -> p c t", t=2048)[
                :, j0:j0 + cnt, i * 128:(i + 1) * 128]
            inv = stage[:, 0:cnt * 128].rearrange("p (c t) -> p c t", t=128)
            return outv, inv
        t_tr = th.run(xn_t, dst_fn, [t_ap, t_id], nblocks=8)
        t_xf_done.append(t_tr)

    dbg = io.get("dbg", {})
    if "xf0" in dbg:
        kb.dma("sp", dbg["xf0"], XF(0), deps=t_xf_done)
    qk_done = []
    pb = 0
    prev_evac = {}
    for part, store in [(p, {"q": QF, "k": KF}[p]) for p in io.get("parts", ("q", "k"))]:
        for j in range(io.get("njq", DC)):
            col0 = (0 if part == "q" else 1024) + j * 128
            for tg in range(io.get("ntg", 4)):
                bank = ps[:, pb * 512:(pb + 1) * 512]
                dep0 = t_xf_done + t_wqk + \
                    ([prev_evac[pb]] if pb in prev_evac else [])
                last_mm = None
                for dd in range(DC):
                    lhsT = WQK(dd)[:, col0:col0 + 128]
                    rhs = XF(dd)[:, tg * 512:(tg + 1) * 512]
                    last_mm = kb.emit("pe", (lambda e, o=bank, l=lhsT, r=rhs,
                                             s=(dd == 0), p=(dd == DC - 1):
                                             e.matmul(o, l, r, start=s, stop=p)),
                                      dep0 if dd == 0 else [],
                                      sem=("pe" if dd == DC - 1 else False))
                bcol = BQC[:, (0 if part == "q" else 8) + j:
                           (0 if part == "q" else 8) + j + 1]
                out = store(j)[:, tg * 512:(tg + 1) * 512]
                tev = kb.emit("act", (lambda e, o=out, b=bank, bc=bcol:
                                      e.activation(o, b, AF.Identity, bias=bc,
                                                   scale=1.0)),
                              [last_mm, t_bq])
                prev_evac[pb] = tev
                qk_done.append(tev)
                pb = (pb + 1) % 4
    if io.get("stop_after") == "qk":
        if "qf0" in io.get("dbg", {}):
            kb.dma("sp", io["dbg"]["qf0"], QF(0), deps=qk_done)
        if "kf0" in io.get("dbg", {}):
            kb.dma("sp", io["dbg"]["kf0"], KF(0), deps=qk_done)
        kb.emit("sp", lambda e: e.nop(), deps=kb.barrier(), sem=False)
        return nc
    t_wv = []
    for dd in range(DC):
        t_wv.append(kb.dma("pool", WV(dd),
                           io["wqkvT"][dd * 128:(dd + 1) * 128, 2048:3072],
                           deps=qk_done))
    v_done = []
    for ti in range(NT):
        for eh in range(2):
            bank = ps[:, pb * 512:(pb + 1) * 512]
            dep0 = t_xf_done + t_wv + ([prev_evac[pb]] if pb in prev_evac else [])
            for dd in range(DC):
                lhsT = XF(dd)[:, ti * 128:(ti + 1) * 128]
                rhs = WV(dd)[:, eh * 512:(eh + 1) * 512]
                kb.emit("pe", (lambda e, o=bank, l=lhsT, r=rhs, s=(dd == 0):
                               e.matmul(o, l, r, start=s, stop=False)),
                        dep0 if dd == 0 else [], sem=False)
            last_mm = kb.emit("pe", (lambda e, o=bank, l=ONE_BF,
                                     r=BVROW[0:1, eh * 512:(eh + 1) * 512]:
                                     e.matmul(o, l, r, start=False, stop=True)),
                              [t_bv, t_one])
            out = VT(ti)[:, eh * 512:(eh + 1) * 512]
            tev = kb.emit("act", (lambda e, o=out, b=bank:
                                  e.activation(o, b, AF.Identity)), [last_mm])
            prev_evac[pb] = tev
            v_done.append(tev)
            pb = (pb + 1) % 4

    if "qf0" in dbg:
        kb.dma("sp", dbg["qf0"], QF(0), deps=qk_done)
    if "qf1" in dbg:
        kb.dma("sp", dbg["qf1"], QF(1), deps=qk_done)
    if "wq0" in dbg:
        kb.dma("sp", dbg["wq0"], WQK(0), deps=qk_done)
    if "kf0" in dbg:
        kb.dma("sp", dbg["kf0"], KF(0), deps=qk_done)
    if "vt0" in dbg:
        kb.dma("sp", dbg["vt0"], VT(0), deps=v_done)
    if io.get("stop_after") == "phase2":
        kb.emit("sp", lambda e: e.nop(), deps=kb.barrier(), sem=False)
        return nc
    t_wo = []
    for dd in range(DC):
        t_wo.append(kb.dma("pool", WO(dd), io["woT"][dd * 128:(dd + 1) * 128, :],
                           deps=v_done))
    attn_barrier = qk_done + v_done

    t_out_final = []
    pt_war = {0: None, 1: None}
    sim_war = None
    pv_war = {0: None, 1: None}
    wo_war = None
    for g in range(4):
        at_done = []
        for jp in range(DC):
            pv_last = {}
            for hh in range(2):
                pt_evacs = []
                for qc in range(4):
                    qt0 = g * 512 + qc * 128
                    simdeps = attn_barrier + ([sim_war] if sim_war else [])
                    last_mm = None
                    for mb in range(4):
                        bank = ps[:, mb * 512:(mb + 1) * 512]
                        lhsT = QF(jp)[hh * 64:(hh + 1) * 64, qt0:qt0 + 128]
                        rhs = KF(jp)[hh * 64:(hh + 1) * 64, mb * 512:(mb + 1) * 512]
                        last_mm = kb.emit("pe", (lambda e, o=bank, l=lhsT, r=rhs:
                                                 e.matmul(o, l, r, start=True, stop=True)),
                                          simdeps if mb == 0 else [],
                                          sem=(False if mb < 3 else "pe"))
                    span = ps[:, 0:2048]
                    t_mx = kb.emit("dve", (lambda e, s=span:
                                           e.reduce_max(DEN[:, 2:3], s, AX.X)),
                                   [last_mm])
                    t_b = kb.emit("dve", (lambda e: e.tensor_scalar(
                        BIA[:, 0:1], DEN[:, 2:3], -SCALE, None, ALU.mult)), [t_mx])
                    t_exp = kb.emit("act", (lambda e, s=span, o=PBUF(hh):
                                            e.activation(o, s, AF.Exp, bias=BIA[:, 0:1],
                                                         scale=SCALE,
                                                         accum_out=DEN[:, 0:1])), [t_b])
                    sim_war = t_exp
                    t_rc = kb.emit("dve", (lambda e:
                                           e.reciprocal(DEN[:, 1:2], DEN[:, 0:1])),
                                   [t_exp])
                    t_nm = kb.emit("dve", (lambda e, o=PBUF(hh):
                                           e.tensor_scalar_mul(o, o, DEN[:, 1:2])),
                                   [t_rc])

                    def dst_fn(j0, cnt, stage, hh=hh, qc=qc):
                        outv = PT(hh).rearrange("p (j c) -> p j c", c=512)[
                            :, j0:j0 + cnt, qc * 128:(qc + 1) * 128]
                        inv = stage[:, 0:cnt * 128].rearrange("p (j c) -> p j c", c=128)
                        return outv, inv
                    if g == 0 and jp == 0 and hh == 0 and qc == 0 and "pn0" in dbg:
                        kb.dma("sp", dbg["pn0"], PBUF(0), deps=[t_nm])
                    dd_tr = [t_nm, t_id] + ([pt_war[hh]] if pt_war[hh] else [])
                    t_tr = th.run(PBUF(hh), dst_fn, dd_tr, nblocks=16)
                    pt_evacs.append(t_tr)
                    if g == 0 and jp == 0 and hh == 0 and qc == 3 and "pt0" in dbg:
                        kb.dma("sp", dbg["pt0"], PT(0), deps=[t_tr])
                pvb = 5 + hh
                bank = ps[:, pvb * 512:(pvb + 1) * 512]
                pvdeps = pt_evacs + ([pv_war[hh]] if pv_war[hh] else [])
                last_mm = None
                for j in range(NT):
                    lhsT = VT(j)[:, (2 * jp + hh) * 64:(2 * jp + hh) * 64 + 64]
                    rhs = PT(hh)[:, j * 512:(j + 1) * 512]
                    last_mm = kb.emit("pe", (lambda e, o=bank[0:64, :], l=lhsT, r=rhs,
                                             s=(j == 0), p=(j == NT - 1):
                                             e.matmul(o, l, r, start=s, stop=p)),
                                      pvdeps if j == 0 else [],
                                      sem=(False if j < NT - 1 else "pe"))
                pv_last[hh] = last_mm
                pt_war[hh] = last_mm
            for hh in range(2):
                bank = ps[:, (5 + hh) * 512:(6 + hh) * 512]
                outap = AT(jp)[0:64, :] if hh == 0 else AT(jp)[64:128, :]
                tev = kb.emit("act", (lambda e, o=outap, b=bank[0:64, :]:
                                      e.activation(o, b, AF.Identity)), [pv_last[hh]])
                pv_war[hh] = tev
                at_done.append(tev)
            if g == 0 and jp == 0 and "at0" in dbg:
                kb.dma("sp", dbg["at0"], AT(0), deps=at_done[-2:])
        for tl in range(4):
            ti = g * 4 + tl
            t_x2 = kb.dma("sp", XT2, io["x"][ti * 128:(ti + 1) * 128, :],
                          deps=[wo_war if wo_war else prev_ln])
            t_xn = kb.emit("dve", (lambda e: e.tensor_scalar(
                XNR, XT2, ST(ti)[:, 2:3], ST(ti)[:, 3:4], ALU.subtract, ALU.mult)),
                [t_x2] + ([wo_war] if wo_war else []))
            t_mix = []
            for eh in range(2):
                bank = ps[:, 7 * 512:8 * 512]
                dep0 = at_done + t_wo + ([wo_war] if wo_war else [])
                for dd in range(DC):
                    lhsT = AT(dd)[:, tl * 128:(tl + 1) * 128]
                    rhs = WO(dd)[:, eh * 512:(eh + 1) * 512]
                    kb.emit("pe", (lambda e, o=bank, l=lhsT, r=rhs, s=(dd == 0):
                                   e.matmul(o, l, r, start=s, stop=False)),
                            dep0 if dd == 0 else [], sem=False)
                last_mm = kb.emit("pe", (lambda e, o=bank, l=ONE_BF,
                                         r=BOROW[0:1, eh * 512:(eh + 1) * 512]:
                                         e.matmul(o, l, r, start=False, stop=True)),
                                  [t_bo, t_one])
                tm = kb.emit("dve", (lambda e, o=MIX[:, eh * 512:(eh + 1) * 512],
                                     b=bank, xn=XNR[:, eh * 512:(eh + 1) * 512]:
                                     e.tensor_add(o, b, xn)), [last_mm, t_xn])
                wo_war = tm
                t_mix.append(tm)
            t_st2 = emit_ln_stats(kb, MIX, SQ, ST2, t_mix)
            t_ap2 = kb.emit("dve", (lambda e: e.tensor_scalar(
                XNR, MIX, ST2[:, 2:3], ST2[:, 3:4], ALU.subtract, ALU.mult)), [t_st2])
            t_add = kb.emit("dve", (lambda e: e.tensor_add(XNR, XNR, XT2)), [t_ap2])
            t_st_out = kb.dma("sp", io["x_out"][ti * 128:(ti + 1) * 128, :], XNR,
                              deps=[t_add])
            t_out_final.append(t_st_out)
            wo_war = t_st_out
    kb.emit("sp", lambda e: e.nop(), deps=t_out_final, sem=False)
    return nc


def host_prep_l0attn(inputs):
    w_qkv = np.asarray(inputs["L0_attn_w_qkv"])
    b_qkv = np.asarray(inputs["L0_attn_b_qkv"])
    w_o = np.asarray(inputs["L0_attn_w_o"])
    b_o = np.asarray(inputs["L0_attn_b_o"])
    return {
        "wqkvT": np.ascontiguousarray(w_qkv.T),
        "bq_cols": np.ascontiguousarray(b_qkv[:2048].reshape(16, 128).T),
        "bv_row": np.ascontiguousarray(b_qkv[2048:].reshape(1, 1024)),
        "woT": np.ascontiguousarray(w_o.T),
        "bo_row": np.ascontiguousarray(b_o.reshape(1, 1024)),
        "ident": host_identity128(),
    }


def make_l0attn_nc():
    nc = bass.Bass("TRN2", target_bir_lowering=False)
    io = {
        "x": nc.dram_tensor("x", [T, D], F32, kind="ExternalInput").ap(),
        "wqkvT": nc.dram_tensor("wqkvT", [1024, 3072], F32, kind="ExternalInput").ap(),
        "bq_cols": nc.dram_tensor("bq_cols", [128, 16], F32, kind="ExternalInput").ap(),
        "bv_row": nc.dram_tensor("bv_row", [1, 1024], F32, kind="ExternalInput").ap(),
        "woT": nc.dram_tensor("woT", [1024, 1024], F32, kind="ExternalInput").ap(),
        "bo_row": nc.dram_tensor("bo_row", [1, 1024], F32, kind="ExternalInput").ap(),
        "ident": nc.dram_tensor("ident", [128, 128], BF16, kind="ExternalInput").ap(),
        "x_out": nc.dram_tensor("x_out", [T, D], F32, kind="ExternalOutput").ap(),
    }
    kb = KB()
    build_l0attn(nc, kb, io)
    kb.finalize(nc)
    return nc






def build_l1attn(nc, kb, io):
    SCALE = 1.0 / 8.0
    CLAMP = 50.0

    a = nc.alloc_sbuf_tensor("Abf", [128, 81920], BF16).ap()
    f = nc.alloc_sbuf_tensor("Af32", [128, 6144], F32).ap()
    ps = nc.alloc_psum_tensor("PS", [128, 4096], F32).ap()

    XF = lambda dd: a[:, dd * 2048:(dd + 1) * 2048]
    QF = lambda dd: a[:, 16384 + dd * 2048:16384 + (dd + 1) * 2048]
    KF = lambda dd: a[:, 32768 + dd * 2048:32768 + (dd + 1) * 2048]
    VT = lambda ti: a[:, 49152 + ti * 1024:49152 + (ti + 1) * 1024]
    WQK = lambda dd: a[:, 65536 + dd * 2048:65536 + (dd + 1) * 2048]
    WV = lambda dd: a[:, 65536 + dd * 1024:65536 + (dd + 1) * 1024]
    WO = lambda dd: a[:, 65536 + dd * 1024:65536 + (dd + 1) * 1024]
    PBUF = lambda hh: a[:, 73728 + hh * 2048:73728 + (hh + 1) * 2048]
    AT = lambda jp: a[:, 77824 + jp * 512:77824 + (jp + 1) * 512]
    PT = lambda hh: a[:, 0 + hh * 8192:(hh + 1) * 8192]

    XT1 = f[:, 0:1024]
    XT2 = f[:, 1024:2048]
    SQ = f[:, 2048:3072]
    MIX = f[:, 3072:4096]
    ST = lambda i: f[:, 4096 + i * 8:4096 + (i + 1) * 8]
    ST2 = f[:, 4224:4232]
    DEN = f[:, 4232:4236]
    BIA = f[:, 4236:4240]
    XNR = f[:, 4240:5264]
    EPS = f[:, 5264:5265]
    CMASK = f[:, 5265:5393]
    kb.epscol = EPS

    IDT = nc.alloc_sbuf_tensor("IDT", [128, 128], BF16).ap()

    t_eps = kb.emit("dve", lambda e: e.memset(EPS, 1e-5))
    t_id = kb.dma("sp", IDT, io["ident"])
    t_cm = kb.dma("sp", CMASK, io["cmask"])
    t_wqk = []
    for dd in range(DC):
        t_wqk.append(kb.dma("pool", WQK(dd)[:, 0:1024],
                            io["wqT"][dd * 128:(dd + 1) * 128, :]))
        t_wqk.append(kb.dma("pool", WQK(dd)[:, 1024:2048],
                            io["wkT"][dd * 128:(dd + 1) * 128, :]))

    th = TransposeHelper(kb, IDT, ps, bank_f32_col=4 * 512)

    t_xf_done = []
    prev_ln = None
    for i in range(NT):
        xt = XT1 if i % 2 == 0 else XT2
        t_x = kb.dma("sp", xt, io["x"][i * 128:(i + 1) * 128, :],
                     deps=[prev_ln] if prev_ln else [])
        t_st = emit_ln_stats(kb, xt, SQ, ST(i), [t_x, t_eps])
        xn_t = QF(0)[:, 0:1024]
        t_ap = emit_ln_apply(kb, xn_t, xt, ST(i), [t_st] + ([th.war] if th.war else []))
        prev_ln = t_ap

        def dst_fn(j0, cnt, stage, i=i):
            outv = a[:, 0:16384].rearrange("p (c t) -> p c t", t=2048)[
                :, j0:j0 + cnt, i * 128:(i + 1) * 128]
            inv = stage[:, 0:cnt * 128].rearrange("p (c t) -> p c t", t=128)
            return outv, inv
        t_tr = th.run(xn_t, dst_fn, [t_ap, t_id], nblocks=8)
        t_xf_done.append(t_tr)

    qk_done = []
    pb = 0
    prev_evac = {}
    for part, store in (("q", QF), ("k", KF)):
        for j in range(DC):
            col0 = (0 if part == "q" else 1024) + j * 128
            for tg in range(4):
                bank = ps[:, pb * 512:(pb + 1) * 512]
                dep0 = t_xf_done + t_wqk + ([prev_evac[pb]] if pb in prev_evac else [])
                last_mm = None
                for dd in range(DC):
                    lhsT = WQK(dd)[:, col0:col0 + 128]
                    rhs = XF(dd)[:, tg * 512:(tg + 1) * 512]
                    last_mm = kb.emit("pe", (lambda e, o=bank, l=lhsT, r=rhs,
                                             s=(dd == 0), p=(dd == DC - 1):
                                             e.matmul(o, l, r, start=s, stop=p)),
                                      dep0 if dd == 0 else [],
                                      sem=("pe" if dd == DC - 1 else False))
                out = store(j)[:, tg * 512:(tg + 1) * 512]
                tev = kb.emit("act", (lambda e, o=out, b=bank:
                                      e.activation(o, b, AF.Identity)), [last_mm])
                prev_evac[pb] = tev
                qk_done.append(tev)
                pb = (pb + 1) % 4
    t_wv = []
    for dd in range(DC):
        t_wv.append(kb.dma("pool", WV(dd), io["wvT"][dd * 128:(dd + 1) * 128, :],
                           deps=qk_done))
    v_done = []
    for ti in range(NT):
        for eh in range(2):
            bank = ps[:, pb * 512:(pb + 1) * 512]
            dep0 = t_xf_done + t_wv + ([prev_evac[pb]] if pb in prev_evac else [])
            last_mm = None
            for dd in range(DC):
                lhsT = XF(dd)[:, ti * 128:(ti + 1) * 128]
                rhs = WV(dd)[:, eh * 512:(eh + 1) * 512]
                last_mm = kb.emit("pe", (lambda e, o=bank, l=lhsT, r=rhs,
                                         s=(dd == 0), p=(dd == DC - 1):
                                         e.matmul(o, l, r, start=s, stop=p)),
                                  dep0 if dd == 0 else [],
                                  sem=("pe" if dd == DC - 1 else False))
            out = VT(ti)[:, eh * 512:(eh + 1) * 512]
            tev = kb.emit("act", (lambda e, o=out, b=bank:
                                  e.activation(o, b, AF.Identity)), [last_mm])
            prev_evac[pb] = tev
            v_done.append(tev)
            pb = (pb + 1) % 4

    t_wo = []
    for dd in range(DC):
        t_wo.append(kb.dma("pool", WO(dd), io["woT"][dd * 128:(dd + 1) * 128, :],
                           deps=v_done))
    attn_barrier = qk_done + v_done

    t_out_final = []
    pt_war = {0: None, 1: None}
    sim_war = None
    pv_war = {0: None, 1: None}
    wo_war = None
    for g in range(4):
        at_done = []
        for jp in range(DC):
            pv_last = {}
            for hh in range(2):
                pt_evacs = []
                for qc in range(4):
                    qci = g * 4 + qc
                    qt0 = qci * 128
                    nm = qci + 1
                    span = nm * 128
                    simdeps = attn_barrier + ([sim_war] if sim_war else [])
                    last_mm = None
                    for mb in range((nm + 3) // 4):
                        n = min(512, span - mb * 512)
                        bank = ps[:, mb * 512:mb * 512 + n]
                        lhsT = QF(jp)[hh * 64:(hh + 1) * 64, qt0:qt0 + 128]
                        rhs = KF(jp)[hh * 64:(hh + 1) * 64, mb * 512:mb * 512 + n]
                        last_mm = kb.emit("pe", (lambda e, o=bank, l=lhsT, r=rhs:
                                                 e.matmul(o, l, r, start=True, stop=True)),
                                          simdeps if mb == 0 else [],
                                          sem=("pe" if (mb + 1) * 512 >= span else False))
                    spanap = ps[:, 0:span]
                    t_th = kb.emit("act", (lambda e, s=spanap:
                                           e.activation(s, s, AF.Tanh,
                                                        scale=SCALE / CLAMP)), [last_mm])
                    t_mk = kb.emit("dve", (lambda e, s=ps[:, span - 128:span]:
                                           e.tensor_add(s, s, CMASK)), [t_th, t_cm])
                    t_mx = kb.emit("dve", (lambda e, s=spanap:
                                           e.reduce_max(DEN[:, 2:3], s, AX.X)), [t_mk])
                    t_b = kb.emit("dve", (lambda e: e.tensor_scalar(
                        BIA[:, 0:1], DEN[:, 2:3], -CLAMP, None, ALU.mult)), [t_mx])
                    t_exp = kb.emit("act", (lambda e, s=spanap, o=PBUF(hh)[:, 0:span]:
                                            e.activation(o, s, AF.Exp, bias=BIA[:, 0:1],
                                                         scale=CLAMP,
                                                         accum_out=DEN[:, 0:1])), [t_b])
                    sim_war = t_exp
                    t_rc = kb.emit("dve", (lambda e:
                                           e.reciprocal(DEN[:, 1:2], DEN[:, 0:1])), [t_exp])
                    t_nm_ = kb.emit("dve", (lambda e, o=PBUF(hh)[:, 0:span]:
                                            e.tensor_scalar_mul(o, o, DEN[:, 1:2])), [t_rc])

                    def dst_fn(j0, cnt, stage, hh=hh, qc=qc):
                        outv = PT(hh).rearrange("p (j c) -> p j c", c=512)[
                            :, j0:j0 + cnt, qc * 128:(qc + 1) * 128]
                        inv = stage[:, 0:cnt * 128].rearrange("p (j c) -> p j c", c=128)
                        return outv, inv
                    dd_tr = [t_nm_, t_id] + ([pt_war[hh]] if pt_war[hh] else [])
                    t_tr = th.run(PBUF(hh), dst_fn, dd_tr, nblocks=nm)
                    pt_evacs.append(t_tr)
                pvb = 5 + hh
                bank = ps[:, pvb * 512:(pvb + 1) * 512]
                pvdeps = pt_evacs + ([pv_war[hh]] if pv_war[hh] else [])
                last_mm = None
                nj = 4 * g + 4
                for j in range(nj):
                    off = max(0, (j - 4 * g)) * 128
                    lhsT = VT(j)[:, (2 * jp + hh) * 64:(2 * jp + hh) * 64 + 64]
                    rhs = PT(hh)[:, j * 512 + off:(j + 1) * 512]
                    last_mm = kb.emit("pe", (lambda e, o=bank[0:64, off:512], l=lhsT, r=rhs,
                                             s=(j == 0), p=(j == nj - 1):
                                             e.matmul(o, l, r, start=s, stop=p)),
                                      pvdeps if j == 0 else [],
                                      sem=(False if j < nj - 1 else "pe"))
                pv_last[hh] = last_mm
                pt_war[hh] = last_mm
            for hh in range(2):
                bank = ps[:, (5 + hh) * 512:(6 + hh) * 512]
                outap = AT(jp)[0:64, :] if hh == 0 else AT(jp)[64:128, :]
                tev = kb.emit("act", (lambda e, o=outap, b=bank[0:64, :]:
                                      e.activation(o, b, AF.Identity)), [pv_last[hh]])
                pv_war[hh] = tev
                at_done.append(tev)
        for tl in range(4):
            ti = g * 4 + tl
            t_x2 = kb.dma("sp", XT2, io["x"][ti * 128:(ti + 1) * 128, :],
                          deps=[wo_war if wo_war else prev_ln])
            t_mix = []
            for eh in range(2):
                bank = ps[:, 7 * 512:8 * 512]
                dep0 = at_done + t_wo + ([wo_war] if wo_war else [])
                last_mm = None
                for dd in range(DC):
                    lhsT = AT(dd)[:, tl * 128:(tl + 1) * 128]
                    rhs = WO(dd)[:, eh * 512:(eh + 1) * 512]
                    last_mm = kb.emit("pe", (lambda e, o=bank, l=lhsT, r=rhs,
                                             s=(dd == 0), p=(dd == DC - 1):
                                             e.matmul(o, l, r, start=s, stop=p)),
                                      dep0 if dd == 0 else [],
                                      sem=("pe" if dd == DC - 1 else False))
                tm = kb.emit("dve", (lambda e, o=MIX[:, eh * 512:(eh + 1) * 512],
                                     b=bank: e.tensor_copy(o, b)), [last_mm])
                wo_war = tm
                t_mix.append(tm)
            t_st2 = emit_ln_stats(kb, MIX, SQ, ST2, t_mix)
            t_ap2 = kb.emit("dve", (lambda e: e.tensor_scalar(
                XNR, MIX, ST2[:, 2:3], ST2[:, 3:4], ALU.subtract, ALU.mult)), [t_st2])
            t_add = kb.emit("dve", (lambda e: e.tensor_add(XNR, XNR, XT2)), [t_ap2])
            t_st_out = kb.dma("sp", io["x_out"][ti * 128:(ti + 1) * 128, :], XNR,
                              deps=[t_add])
            t_out_final.append(t_st_out)
            wo_war = t_st_out
    kb.emit("sp", lambda e: e.nop(), deps=t_out_final, sem=False)
    return nc


def host_prep_l1attn(inputs):
    return {
        "wqT": np.ascontiguousarray(np.asarray(inputs["L1_attn_Wq"]).T),
        "wkT": np.ascontiguousarray(np.asarray(inputs["L1_attn_Wk"]).T),
        "wvT": np.ascontiguousarray(np.asarray(inputs["L1_attn_Wv"]).T),
        "woT": np.ascontiguousarray(np.asarray(inputs["L1_attn_Wo"]).T),
        "ident": host_identity128(),
        "cmask": host_causal_diag_mask(),
    }


def make_l1attn_nc():
    nc = bass.Bass("TRN2", target_bir_lowering=False)
    io = {
        "x": nc.dram_tensor("x", [T, D], F32, kind="ExternalInput").ap(),
        "wqT": nc.dram_tensor("wqT", [1024, 1024], F32, kind="ExternalInput").ap(),
        "wkT": nc.dram_tensor("wkT", [1024, 1024], F32, kind="ExternalInput").ap(),
        "wvT": nc.dram_tensor("wvT", [1024, 1024], F32, kind="ExternalInput").ap(),
        "woT": nc.dram_tensor("woT", [1024, 1024], F32, kind="ExternalInput").ap(),
        "ident": nc.dram_tensor("ident", [128, 128], BF16, kind="ExternalInput").ap(),
        "cmask": nc.dram_tensor("cmask", [128, 128], F32, kind="ExternalInput").ap(),
        "x_out": nc.dram_tensor("x_out", [T, D], F32, kind="ExternalOutput").ap(),
    }
    kb = KB()
    build_l1attn(nc, kb, io)
    kb.finalize(nc)
    return nc



def build_l1ffn(nc, kb, io):
    a = nc.alloc_sbuf_tensor("Abf", [128, 65664], BF16).ap()
    f = nc.alloc_sbuf_tensor("Af32", [128, 14336], F32).ap()
    ps = nc.alloc_psum_tensor("PS", [128, 4096], F32).ap()

    XF = lambda dd: a[:, dd * 2048:(dd + 1) * 2048]
    HFM = lambda fj: a[:, 16384 + fj * 1024:16384 + (fj + 1) * 1024]
    MID = lambda fj: a[:, 24576 + fj * 1024:24576 + (fj + 1) * 1024]
    W1B = lambda s, dd: a[:, 32768 + s * 8192 + dd * 1024:32768 + s * 8192 + (dd + 1) * 1024]
    W2B = lambda s, fj: a[:, 49152 + s * 8192 + fj * 1024:49152 + s * 8192 + (fj + 1) * 1024]
    SGU = lambda dd: a[:, 32768 + dd * 2048:32768 + (dd + 1) * 2048]
    SDN = lambda fj: a[:, 49152 + fj * 1024:49152 + (fj + 1) * 1024]
    RW = lambda dd: a[:, 65536 + dd * 8:65536 + (dd + 1) * 8]
    SGW = lambda dd: a[:, 65600 + dd:65600 + dd + 1]

    XT1 = f[:, 0:1024]
    XT2 = f[:, 1024:2048]
    SQ = f[:, 2048:3072]
    MIX = f[:, 3072:4096]
    ST = lambda i: f[:, 4096 + i * 8:4096 + (i + 1) * 8]
    ST2 = f[:, 4224:4232]
    XNR = f[:, 4240:5264]
    EPS = f[:, 5264:5265]
    RL = f[:, 5280:5288]
    PRB = f[:, 5288:5296]
    TMP8 = f[:, 5296:5304]
    SC = f[:, 5304:5312]
    CMB = f[:, 5312:5440]
    SGC = f[:, 5440:5456]
    RACC = f[:, 6144:14336]
    kb.epscol = EPS

    IDT = nc.alloc_sbuf_tensor("IDT", [128, 128], BF16).ap()

    t_eps = kb.emit("dve", lambda e: e.memset(EPS, 1e-5))
    t_id = kb.dma("sp", IDT, io["ident"])
    RWF = f[:, 5456:5520]
    SGWF = f[:, 5520:5528]
    t_rw0 = kb.dma("sp", RWF, io["router_w_cols"])
    t_sgw0 = kb.dma("sp", SGWF, io["sgw_cols"])
    t_rw = [kb.emit("dve", lambda e: e.tensor_copy(a[:, 65536:65600], RWF), [t_rw0])]
    t_sgw = [kb.emit("dve", lambda e: e.tensor_copy(a[:, 65600:65608], SGWF), [t_sgw0])]

    th = TransposeHelper(kb, IDT, ps, bank_f32_col=4 * 512)

    t_xf_done = []
    prev_ln = None
    for i in range(NT):
        xt = XT1 if i % 2 == 0 else XT2
        t_x = kb.dma("sp", xt, io["x"][i * 128:(i + 1) * 128, :],
                     deps=[prev_ln] if prev_ln else [])
        t_st = emit_ln_stats(kb, xt, SQ, ST(i), [t_x, t_eps])
        xn_t = HFM(0)[:, 0:1024]
        t_ap = emit_ln_apply(kb, xn_t, xt, ST(i), [t_st] + ([th.war] if th.war else []))
        prev_ln = t_ap

        def dst_fn(j0, cnt, stage, i=i):
            outv = a[:, 0:16384].rearrange("p (c t) -> p c t", t=2048)[
                :, j0:j0 + cnt, i * 128:(i + 1) * 128]
            inv = stage[:, 0:cnt * 128].rearrange("p (c t) -> p c t", t=128)
            return outv, inv
        t_tr = th.run(xn_t, dst_fn, [t_ap, t_id], nblocks=8)
        t_xf_done.append(t_tr)

    cmb_done = []
    sg_done = []
    rt_war = None
    for i in range(NT):
        bank = ps[:, 0:8]
        dep0 = t_xf_done + t_rw + ([rt_war] if rt_war else [])
        last_mm = None
        for dd in range(DC):
            lhsT = XF(dd)[:, i * 128:(i + 1) * 128]
            last_mm = kb.emit("pe", (lambda e, o=bank, l=lhsT, r=RW(dd),
                                     s=(dd == 0), p=(dd == DC - 1):
                                     e.matmul(o, l, r, start=s, stop=p)),
                              dep0 if dd == 0 else [],
                              sem=("pe" if dd == DC - 1 else False))
        t_rl = kb.emit("act", (lambda e: e.activation(RL, ps[:, 0:8], AF.Identity)),
                       [last_mm])
        if i == 0 and "rl0" in io.get("dbg", {}):
            kb.dma("sp", io["dbg"]["rl0"], RL, deps=[t_rl])
        rt_war = t_rl
        bank2 = ps[:, 512:513]
        dep1 = t_xf_done + t_sgw + ([rt_war] if rt_war else [])
        last2 = None
        for dd in range(DC):
            lhsT = XF(dd)[:, i * 128:(i + 1) * 128]
            last2 = kb.emit("pe", (lambda e, o=bank2, l=lhsT, r=SGW(dd),
                                   s=(dd == 0), p=(dd == DC - 1):
                                   e.matmul(o, l, r, start=s, stop=p)),
                            dep1 if dd == 0 else [],
                            sem=("pe" if dd == DC - 1 else False))
        t_sg = kb.emit("act", (lambda e, o=SGC[:, i:i + 1]:
                               e.activation(o, ps[:, 512:513], AF.Sigmoid)), [last2])
        sg_done.append(t_sg)
        t1 = kb.emit("dve", (lambda e: e.reduce_max(SC[:, 0:1], RL, AX.X)), [t_rl])
        t2 = kb.emit("dve", (lambda e: e.tensor_scalar_mul(SC[:, 1:2], SC[:, 0:1], -1.0)), [t1])
        t3 = kb.emit("act", (lambda e: e.activation(PRB, RL, AF.Exp, bias=SC[:, 1:2],
                                                    scale=1.0, accum_out=SC[:, 2:3])), [t2])
        t4 = kb.emit("dve", (lambda e: e.reciprocal(SC[:, 3:4], SC[:, 2:3])), [t3])
        t5 = kb.emit("dve", (lambda e: e.tensor_scalar_mul(PRB, PRB, SC[:, 3:4])), [t4])
        t6 = kb.emit("dve", (lambda e: e.reduce_max(SC[:, 4:5], PRB, AX.X)), [t5])
        t7 = kb.emit("dve", (lambda e: e.tensor_scalar(TMP8, PRB, SC[:, 4:5], None,
                                                       ALU.is_equal)), [t6])
        t8 = kb.emit("dve", (lambda e: e.scalar_tensor_tensor(TMP8, TMP8, -1e30, PRB,
                                                              ALU.mult, ALU.add)), [t7])
        t9 = kb.emit("dve", (lambda e: e.reduce_max(SC[:, 5:6], TMP8, AX.X)), [t8])
        t10 = kb.emit("dve", (lambda e: e.tensor_scalar(TMP8, PRB, SC[:, 5:6], None,
                                                        ALU.is_ge)), [t9])
        t11 = kb.emit("dve", (lambda e, o=CMB[:, i * 8:(i + 1) * 8]:
                              e.tensor_mul(o, PRB, TMP8)), [t10])
        cmb_done.append(t11)

    prev_evac = {}
    dbg = io.get("dbg", {})
    if "rw" in dbg:
        kb.dma("sp", dbg["rw"], a[:, 65536:65608], deps=t_rw + t_sgw)
    if "cmb" in dbg:
        kb.dma("sp", dbg["cmb"], CMB, deps=cmb_done)
    if "sgc" in dbg:
        kb.dma("sp", dbg["sgc"], SGC, deps=cmb_done + kb.barrier())
    for pbx in range(4):
        prev_evac[pbx] = sg_done[-1]
    racc_done = [None] * 16
    w_war = {("w1", 0): None, ("w1", 1): None, ("w2", 0): None, ("w2", 1): None}
    pb = 0
    hf_war = None
    y_war = {}
    racc_last = {}
    for half in range(2):
        for e_i in range(8):
            s = e_i % 2
            d1 = [w_war[("w1", s)]] if w_war[("w1", s)] else []
            t_w1 = [kb.dma("pool", W1B(s, dd),
                           io["w1"][e_i, dd * 128:(dd + 1) * 128, :], deps=d1)
                    for dd in range(DC)]
            d2 = [w_war[("w2", s)]] if w_war[("w2", s)] else []
            t_w2 = [kb.dma("pool", W2B(s, fj),
                           io["w2"][e_i, fj * 128:(fj + 1) * 128, :], deps=d2)
                    for fj in range(DC)]
            h_done = []
            for fj in range(DC):
                for tg in range(2):
                    bank = ps[:, pb * 512:(pb + 1) * 512]
                    dep0 = t_xf_done + t_w1 + ([prev_evac[pb]] if pb in prev_evac else [])
                    last_mm = None
                    for dd in range(DC):
                        lhsT = W1B(s, dd)[:, fj * 128:(fj + 1) * 128]
                        rhs = XF(dd)[:, half * 1024 + tg * 512:half * 1024 + (tg + 1) * 512]
                        last_mm = kb.emit("pe", (lambda e, o=bank, l=lhsT, r=rhs,
                                                 st=(dd == 0), sp=(dd == DC - 1):
                                                 e.matmul(o, l, r, start=st, stop=sp)),
                                          dep0 if dd == 0 else [],
                                          sem=("pe" if dd == DC - 1 else False))
                    out = HFM(fj)[:, tg * 512:(tg + 1) * 512]
                    tev = kb.emit("act", (lambda e, o=out, b=bank:
                                          e.activation(o, b, AF.Silu)), [last_mm])
                    prev_evac[pb] = tev
                    h_done.append(tev)
                    pb = (pb + 1) % 4
            w_war[("w1", s)] = h_done[-1]
            for tl in range(8):
                gt = half * 8 + tl
                for oh in range(2):
                    bank = ps[:, (4 + (pb % 2)) * 512:(5 + (pb % 2)) * 512]
                    key = 4 + (pb % 2)
                    dep0 = h_done + t_w2 + ([y_war.get(key)] if y_war.get(key) else [])
                    last_mm = None
                    for fj in range(DC):
                        lhsT = HFM(fj)[:, tl * 128:(tl + 1) * 128]
                        rhs = W2B(s, fj)[:, oh * 512:(oh + 1) * 512]
                        last_mm = kb.emit("pe", (lambda e, o=bank, l=lhsT, r=rhs,
                                                 st=(fj == 0), sp=(fj == DC - 1):
                                                 e.matmul(o, l, r, start=st, stop=sp)),
                                          dep0 if fj == 0 else [],
                                          sem=("pe" if fj == DC - 1 else False))
                    rs = RACC[:, tl * 1024 + oh * 512:tl * 1024 + (oh + 1) * 512]
                    ccol = CMB[:, gt * 8 + e_i:gt * 8 + e_i + 1]
                    rdeps = [last_mm, cmb_done[gt]] + \
                        ([racc_last.get((tl, oh))] if racc_last.get((tl, oh)) else [])
                    if e_i == 0:
                        tev = kb.emit("dve", (lambda e, o=rs, b=bank, c=ccol:
                                              e.tensor_scalar_mul(o, b, c)), rdeps)
                    else:
                        tev = kb.emit("dve", (lambda e, o=rs, b=bank, c=ccol:
                                              e.scalar_tensor_tensor(o, b, c, o,
                                                                     ALU.mult, ALU.add)),
                                      rdeps)
                    racc_last[(tl, oh)] = tev
                    y_war[key] = tev
                    pb = (pb + 1) % 4
            w_war[("w2", s)] = y_war[4] if 4 in y_war else None

        if half == 0 and "racc" in dbg:
            kb.dma("sp", dbg["racc"], RACC, deps=[racc_last[k] for k in racc_last])
        bar = kb.barrier()
        t_sgu = [kb.dma("pool", SGU(dd), io["shared_gate_up"][dd * 128:(dd + 1) * 128, :],
                        deps=bar) for dd in range(DC)]
        t_sdn = [kb.dma("pool", SDN(fj), io["shared_down"][fj * 128:(fj + 1) * 128, :],
                        deps=bar) for fj in range(DC)]
        mid_done = []
        gs_war = None
        for fi in range(DC):
            for tg in range(2):
                bankA = ps[:, 0:512]
                bankB = ps[:, 512:1024]
                depA = t_xf_done + t_sgu + ([prev_evac.get("sgA")] if prev_evac.get("sgA") else [])
                lastA = None
                for dd in range(DC):
                    lhsT = SGU(dd)[:, fi * 128:(fi + 1) * 128]
                    rhs = XF(dd)[:, half * 1024 + tg * 512:half * 1024 + (tg + 1) * 512]
                    lastA = kb.emit("pe", (lambda e, o=bankA, l=lhsT, r=rhs,
                                           st=(dd == 0), sp=(dd == DC - 1):
                                           e.matmul(o, l, r, start=st, stop=sp)),
                                    depA if dd == 0 else [],
                                    sem=("pe" if dd == DC - 1 else False))
                depB = ([prev_evac.get("sgB")] if prev_evac.get("sgB") else [])
                lastB = None
                for dd in range(DC):
                    lhsT = SGU(dd)[:, 1024 + fi * 128:1024 + (fi + 1) * 128]
                    rhs = XF(dd)[:, half * 1024 + tg * 512:half * 1024 + (tg + 1) * 512]
                    lastB = kb.emit("pe", (lambda e, o=bankB, l=lhsT, r=rhs,
                                           st=(dd == 0), sp=(dd == DC - 1):
                                           e.matmul(o, l, r, start=st, stop=sp)),
                                    depB if dd == 0 else [],
                                    sem=("pe" if dd == DC - 1 else False))
                gs = HFM(0)[:, 0:512]
                t_gs = kb.emit("act", (lambda e, o=gs, b=bankA:
                                       e.activation(o, b, AF.Silu)),
                               [lastA] + ([gs_war] if gs_war else []))
                prev_evac["sgA"] = t_gs
                out = MID(fi)[:, tg * 512:(tg + 1) * 512]
                t_md = kb.emit("dve", (lambda e, o=out, g=gs, b=bankB:
                                       e.tensor_mul(o, g, b)), [t_gs, lastB])
                gs_war = t_md
                prev_evac["sgB"] = t_md
                mid_done.append(t_md)
        out_done = []
        mix_war = None
        for tl in range(8):
            gt = half * 8 + tl
            t_x2 = kb.dma("sp", XT2, io["x"][gt * 128:(gt + 1) * 128, :],
                          deps=[mix_war if mix_war else prev_ln])
            t_mix = []
            for oh in range(2):
                bank = ps[:, 6 * 512:7 * 512]
                dep0 = mid_done + t_sdn + ([mix_war] if mix_war else [])
                last_mm = None
                for fj in range(DC):
                    lhsT = MID(fj)[:, tl * 128:(tl + 1) * 128]
                    rhs = SDN(fj)[:, oh * 512:(oh + 1) * 512]
                    last_mm = kb.emit("pe", (lambda e, o=bank, l=lhsT, r=rhs,
                                             st=(fj == 0), sp=(fj == DC - 1):
                                             e.matmul(o, l, r, start=st, stop=sp)),
                                      dep0 if fj == 0 else [],
                                      sem=("pe" if fj == DC - 1 else False))
                rs = RACC[:, tl * 1024 + oh * 512:tl * 1024 + (oh + 1) * 512]
                tm = kb.emit("dve", (lambda e, o=MIX[:, oh * 512:(oh + 1) * 512],
                                     b=bank, c=SGC[:, gt:gt + 1], r=rs:
                                     e.scalar_tensor_tensor(o, b, c, r, ALU.mult, ALU.add)),
                             [last_mm, racc_last[(tl, oh)]])
                mix_war = tm
                t_mix.append(tm)
            t_st2 = emit_ln_stats(kb, MIX, SQ, ST2, t_mix)
            t_ap2 = kb.emit("dve", (lambda e: e.tensor_scalar(
                XNR, MIX, ST2[:, 2:3], ST2[:, 3:4], ALU.subtract, ALU.mult)), [t_st2])
            t_add = kb.emit("dve", (lambda e: e.tensor_add(XNR, XNR, XT2)), [t_ap2])
            t_out = kb.dma("sp", io["x_out"][gt * 128:(gt + 1) * 128, :], XNR,
                           deps=[t_add])
            out_done.append(t_out)
            mix_war = t_out
        for k in racc_last:
            racc_last[k] = None
        racc_last = {}
        for k in list(w_war):
            w_war[k] = out_done[-1]
        prev_evac = {}
        y_war = {}
    kb.emit("sp", lambda e: e.nop(), deps=kb.barrier(), sem=False)
    return nc


def host_prep_l1ffn(inputs):
    return {
        "router_w_cols": np.ascontiguousarray(
            np.asarray(inputs["L1_ffn_router_w"]).reshape(8, 128, 8).transpose(1, 0, 2).reshape(128, 64)),
        "sgw_cols": np.ascontiguousarray(
            np.asarray(inputs["L1_ffn_shared_gate_w"]).reshape(8, 128).T),
        "w1": np.ascontiguousarray(np.asarray(inputs["L1_ffn_w1"])),
        "w2": np.ascontiguousarray(np.asarray(inputs["L1_ffn_w2"])),
        "shared_gate_up": np.ascontiguousarray(np.asarray(inputs["L1_ffn_shared_gate_up"])),
        "shared_down": np.ascontiguousarray(np.asarray(inputs["L1_ffn_shared_down"])),
        "ident": host_identity128(),
    }


def make_l1ffn_nc():
    nc = bass.Bass("TRN2", target_bir_lowering=False)
    io = {
        "x": nc.dram_tensor("x", [T, D], F32, kind="ExternalInput").ap(),
        "router_w_cols": nc.dram_tensor("router_w_cols", [128, 64], F32, kind="ExternalInput").ap(),
        "sgw_cols": nc.dram_tensor("sgw_cols", [128, 8], F32, kind="ExternalInput").ap(),
        "w1": nc.dram_tensor("w1", [8, 1024, 1024], F32, kind="ExternalInput").ap(),
        "w2": nc.dram_tensor("w2", [8, 1024, 1024], F32, kind="ExternalInput").ap(),
        "shared_gate_up": nc.dram_tensor("shared_gate_up", [1024, 2048], F32, kind="ExternalInput").ap(),
        "shared_down": nc.dram_tensor("shared_down", [1024, 1024], F32, kind="ExternalInput").ap(),
        "ident": nc.dram_tensor("ident", [128, 128], BF16, kind="ExternalInput").ap(),
        "x_out": nc.dram_tensor("x_out", [T, D], F32, kind="ExternalOutput").ap(),
    }
    kb = KB()
    build_l1ffn(nc, kb, io)
    kb.finalize(nc)
    return nc



def build_l0f_attn(nc, kb, io):
    QK_SCALE = 8.0

    a = nc.alloc_sbuf_tensor("Abf", [128, 81920], BF16).ap()
    f = nc.alloc_sbuf_tensor("Af32", [128, 5696], F32).ap()
    ps = nc.alloc_psum_tensor("PS", [128, 4096], F32).ap()

    TBLF = lambda k: a[:, k * 2048:(k + 1) * 2048]
    CSQ = a[:, 8192:10240]
    SNQ = a[:, 10240:12288]
    CSK = a[:, 12288:14336]
    SNK = a[:, 14336:16384]
    XS = lambda dd: a[:, dd * 2048:(dd + 1) * 2048]
    QFC = lambda dd: a[:, dd * 2048:(dd + 1) * 2048]
    QT = lambda ti: a[:, 16384 + ti * 1024:16384 + (ti + 1) * 1024]
    KFC = lambda dd: a[:, 16384 + dd * 2048:16384 + (dd + 1) * 2048]
    KT = lambda ti: a[:, 32768 + ti * 1024:32768 + (ti + 1) * 1024]
    PTh = lambda hh: a[:, 32768 + hh * 6144:32768 + (hh + 1) * 6144]
    ATT = a[:, 45056:47104]
    VT = lambda ti: a[:, 49152 + ti * 1024:49152 + (ti + 1) * 1024]
    WQ = lambda s, dd: a[:, 65536 + s * 4096 + dd * 512:65536 + s * 4096 + (dd + 1) * 512]
    WOC = lambda dd: a[:, 65536 + dd * 1024:65536 + (dd + 1) * 1024]
    TBL = lambda k, ti: a[:, 77824 + k * 1024 + ti * 64:77824 + k * 1024 + (ti + 1) * 64]
    ATG = lambda jp: a[:, 73728 + jp * 512:73728 + (jp + 1) * 512]

    XT1 = f[:, 0:1024]
    XT2 = f[:, 1024:2048]
    SQ = f[:, 2048:3072]
    MSK = lambda c: f[:, 3072 + c * 384:3072 + (c + 1) * 384]
    QSC = f[:, 4224:4226]
    NRM = f[:, 4240:4256]
    DEN = f[:, 4256:4260]
    BIA = f[:, 4260:4264]
    PRM = f[:, 4264:4265]
    MSKT = f[:, 4272:4656]
    XNR = f[:, 4656:5680]

    IDT = nc.alloc_sbuf_tensor("IDT", [128, 128], BF16).ap()

    t_id = kb.dma("sp", IDT, io["ident"])
    t_tb = [kb.dma("sp", CSQ[0:64, :], io["cs_q_fm"]), kb.dma("sp", SNQ[0:64, :], io["sn_q_fm"]),
            kb.dma("sp", CSK[0:64, :], io["cs_k_fm"]), kb.dma("sp", SNK[0:64, :], io["sn_k_fm"])]
    t_ms = [kb.dma("sp", MSK(c), io["masks"][c]) for c in range(3)]
    t_qs = kb.dma("sp", QSC[0:64, :], io["qs_cols"])

    th = TransposeHelper(kb, IDT, ps, bank_f32_col=4 * 512)

    tb_done = []
    for k, srcF in enumerate((CSQ, SNQ, CSK, SNK)):
        col = 0 if k < 2 else 1
        if k in (1, 3):
            t_p1 = kb.emit("dve", (lambda e, c=col: e.tensor_copy(
                PRM[0:32, :], QSC[32:64, c:c + 1])), [t_qs])
            t_p2 = kb.emit("dve", (lambda e, c=col: e.tensor_copy(
                PRM[32:64, :], QSC[0:32, c:c + 1])), [t_qs, t_p1])
            scol = PRM[0:64, :]
            dep = [t_p1, t_p2] + t_tb
        else:
            scol = QSC[0:64, col:col + 1]
            dep = [t_qs] + t_tb
        t_f = kb.emit("dve", (lambda e, o=TBLF(k)[0:64, :], s=srcF[0:64, :], c=scol:
                              e.tensor_scalar_mul(o, s, c)), dep)
        for ti in range(NT):
            blk = TBLF(k)[0:64, ti * 128:(ti + 1) * 128]
            ttr = kb.emit("pe", (lambda e, s=blk: e.transpose(
                th.stage[0:128, 0:64], s, IDT[0:64, 0:64])),
                [t_f, t_id] + ([th.war] if th.war else []))
            th.war = kb.emit("dve", (lambda e, o=TBL(k, ti):
                                     e.tensor_copy(o, th.stage[0:128, 0:64])), [ttr])
            tb_done.append(th.war)

    t_xs_done = []
    prev_ln = None
    for i in range(NT):
        xt = XT1 if i % 2 == 0 else XT2
        d0 = ([prev_ln] if prev_ln else []) + tb_done
        t_lo = kb.dma("sp", xt[:, 0:512], io["x"][i * 128:(i + 1) * 128, 0:512], deps=d0)
        if i == 0:
            t_h1 = kb.dma("sp", xt[1:128, 512:1024], io["x"][0:127, 512:1024], deps=d0)
            t_z = kb.emit("dve", (lambda e, x=xt: e.memset(x[0:1, 512:1024], 0.0)), d0)
            t_hi = [t_h1, t_z]
        else:
            t_hi = [kb.dma("sp", xt[:, 512:1024],
                           io["x"][i * 128 - 1:(i + 1) * 128 - 1, 512:1024], deps=d0)]
        xn_t = a[:, 16384:17408]
        t_c = kb.emit("dve", (lambda e, o=xn_t, x=xt: e.tensor_copy(o, x)),
                      [t_lo] + t_hi + ([th.war] if th.war else []))
        prev_ln = t_c

        def dst_fn(j0, cnt, stage, i=i):
            outv = a[:, 0:16384].rearrange("p (c t) -> p c t", t=2048)[
                :, j0:j0 + cnt, i * 128:(i + 1) * 128]
            inv = stage[:, 0:cnt * 128].rearrange("p (c t) -> p c t", t=128)
            return outv, inv
        t_tr = th.run(xn_t, dst_fn, [t_c, t_id] + tb_done, nblocks=8)
        t_xs_done.append(t_tr)

    dbg = io.get("dbg", {})
    if "xs0" in dbg:
        kb.dma("sp", dbg["xs0"], a[:, 0:2048], deps=t_xs_done)
    qkv_done = {"q": [], "k": [], "v": []}
    pb = 0
    prev_evac = {}
    w_war = {0: None, 1: None}
    store = (QT, QT, KT, KT, VT, VT)
    for n in range(6):
        s = n % 2
        d1 = [w_war[s]] if w_war[s] else []
        t_w = [kb.dma("pool", WQ(s, dd),
                      io["wqkv"][dd * 128:(dd + 1) * 128, n * 512:(n + 1) * 512],
                      deps=d1) for dd in range(DC)]
        part = ("q", "q", "k", "k", "v", "v")[n]
        last_of_n = None
        for ti in range(NT):
            bank = ps[:, pb * 512:(pb + 1) * 512]
            dep0 = t_xs_done + t_w + ([prev_evac[pb]] if pb in prev_evac else [])
            last_mm = None
            for dd in range(DC):
                lhsT = XS(dd)[:, ti * 128:(ti + 1) * 128]
                rhs = WQ(s, dd)
                last_mm = kb.emit("pe", (lambda e, o=bank, l=lhsT, r=rhs,
                                         st=(dd == 0), sp=(dd == DC - 1):
                                         e.matmul(o, l, r, start=st, stop=sp)),
                                  dep0 if dd == 0 else [],
                                  sem=("pe" if dd == DC - 1 else False))
            out = store[n](ti)[:, (n % 2) * 512:(n % 2) * 512 + 512]
            tev = kb.emit("act", (lambda e, o=out, b=bank:
                                  e.activation(o, b, AF.Identity)), [last_mm])
            prev_evac[pb] = tev
            qkv_done[part].append(tev)
            last_of_n = tev
            pb = (pb + 1) % 4
        w_war[s] = last_of_n

    if "qt0" in dbg:
        kb.dma("sp", dbg["qt0"], a[:, 16384:17408], deps=qkv_done["q"])
    if "vt0" in dbg:
        kb.dma("sp", dbg["vt0"], a[:, 49152:50176], deps=qkv_done["v"])
    rot_done = {"q": [], "k": []}
    for part, TM, fmbase in (("q", QT, 0), ("k", KT, 16384)):
        tblc, tbls = (0, 1) if part == "q" else (2, 3)
        for ti in range(NT):
            tm = TM(ti)
            deps0 = qkv_done[part] + tb_done + \
                (rot_done["q"][-1:] if part == "k" else [])
            t_sq = kb.emit("dve", (lambda e, t=tm: e.tensor_mul(SQ, t, t)), deps0)
            t_rd = kb.emit("dve", (lambda e: e.reduce_sum(
                NRM, SQ.rearrange("p (h c) -> p h c", c=64), AX.X)), [t_sq])
            t_sd = kb.emit("act", (lambda e: e.activation(NRM, NRM, AF.Sqrt)), [t_rd])
            t_cl = kb.emit("dve", (lambda e: e.tensor_scalar_max(NRM, NRM, 1e-12)), [t_sd])
            t_rc = kb.emit("dve", (lambda e: e.reciprocal(NRM, NRM)), [t_cl])
            t_nh = None
            for h in range(16):
                t_nh = kb.emit("dve", (lambda e, t=tm[:, h * 64:(h + 1) * 64],
                                       c=NRM[:, h:h + 1]:
                                       e.tensor_scalar_mul(t, t, c)),
                               [t_rc] if h == 0 else [])
            A2 = TBL(tblc, ti)
            B2 = TBL(tbls, ti)
            ROTT = a[:, 65536:67584]
            t_m1 = None
            for h in range(16):
                hs = slice(h * 64, (h + 1) * 64)
                t_m1 = kb.emit("pool", (lambda e, o=ROTT[:, 0 + h * 64:0 + (h + 1) * 64],
                                        t=tm[:, hs], Ax=A2:
                                        e.tensor_mul(o, t, Ax)),
                               [t_nh] if h == 0 else [])
                t_m1 = kb.emit("pool", (lambda e,
                                        o=ROTT[:, 1024 + h * 64:1024 + h * 64 + 32],
                                        t=tm[:, h * 64 + 32:h * 64 + 64],
                                        Bx=B2[:, 0:32]:
                                        e.tensor_mul(o, t, Bx)), [])
                t_m1 = kb.emit("pool", (lambda e,
                                        o=ROTT[:, 1024 + h * 64 + 32:1024 + (h + 1) * 64],
                                        t=tm[:, h * 64:h * 64 + 32],
                                        Bx=B2[:, 32:64]:
                                        e.tensor_mul(o, t, Bx)), [])
            t_m4 = kb.emit("pool", (lambda e, t=tm: e.tensor_add(
                t, ROTT[:, 0:1024], ROTT[:, 1024:2048])), [t_m1])

            def dst_fn(j0, cnt, stage, i=ti, base=fmbase):
                outv = a[:, base:base + 16384].rearrange("p (c t) -> p c t", t=2048)[
                    :, j0:j0 + cnt, i * 128:(i + 1) * 128]
                inv = stage[:, 0:cnt * 128].rearrange("p (c t) -> p c t", t=128)
                return outv, inv
            ddep = [t_m4, t_id]
            if part == "q":
                ddep = ddep + t_xs_done + qkv_done["q"] + qkv_done["k"] + qkv_done["v"]
            else:
                ddep = ddep + qkv_done["k"]
            t_tr = th.run(tm, dst_fn, ddep, nblocks=8)
            rot_done[part].append(t_tr)
            if part == "q" and ti == 0 and "qr0" in dbg:
                kb.dma("sp", dbg["qr0"], TM(0), deps=[t_m4])
            if part == "q" and ti == 0 and "nrm0" in dbg:
                kb.dma("sp", dbg["nrm0"], NRM, deps=[t_rc])
            if part == "q" and ti == 0 and "tbl00" in dbg:
                kb.dma("sp", dbg["tbl00"], TBL(2, 0), deps=tb_done)
                kb.dma("sp", dbg["tbl10"], TBL(3, 0), deps=tb_done)
            if part == "k" and ti == 15 and "kr0" in dbg:
                kb.dma("sp", dbg["kr0"], TM(15), deps=[t_m4])

    if "qf0" in dbg:
        kb.dma("sp", dbg["qf0"], a[:, 0:2048], deps=rot_done["q"] + rot_done["k"])
    if "kf0" in dbg:
        kb.dma("sp", dbg["kf0"], a[:, 16384:18432], deps=rot_done["q"] + rot_done["k"])
    if io.get("stop_after") == "rot":
        kb.emit("sp", lambda e: e.nop(), deps=kb.barrier(), sem=False)
        return nc
    t_wo = [kb.dma("pool", WOC(dd), io["wo"][dd * 128:(dd + 1) * 128, :],
                   deps=qkv_done["v"] + rot_done["q"] + rot_done["k"])
            for dd in range(DC)]
    barrier_attn = rot_done["q"] + rot_done["k"] + qkv_done["v"] + tb_done

    sim_war = None
    pt_war = None
    pv_war = {0: None, 1: None}
    exp_war = None
    wo_war = None
    outs = []
    for g in range(4):
        at_done = []
        for jp in range(DC):
            pt_evacs = []
            for hh in range(2):
                for qc in range(4):
                    qi = g * 4 + qc
                    kb0 = max(0, qi * 128 - 256)
                    span = 384 if qi >= 2 else (128 if qi == 0 else 256)
                    case = qi if qi < 2 else 2
                    bank = ps[:, (qc % 4) * 512:(qc % 4) * 512 + span]
                    simdeps = barrier_attn + ([sim_war] if sim_war else [])
                    lhsT = QFC(jp)[hh * 64:(hh + 1) * 64, qi * 128:(qi + 1) * 128]
                    rhs = KFC(jp)[hh * 64:(hh + 1) * 64, kb0:kb0 + span]
                    t_mm = kb.emit("pe", (lambda e, o=bank, l=lhsT, r=rhs:
                                          e.matmul(o, l, r, start=True, stop=True)),
                                   simdeps)
                    t_ma = kb.emit("dve", (lambda e, o=MSKT[:, 0:span], b=bank,
                                           m=MSK(case)[:, 0:span]:
                                           e.tensor_add(o, b, m)), [t_mm] + t_ms)
                    sim_war = t_ma
                    t_mx = kb.emit("dve", (lambda e, s=MSKT[:, 0:span]:
                                           e.reduce_max(DEN[:, 2:3], s, AX.X)), [t_ma])
                    t_b = kb.emit("dve", (lambda e: e.tensor_scalar(
                        BIA[:, 0:1], DEN[:, 2:3], -QK_SCALE, None, ALU.mult)), [t_mx])
                    PB = ATT[:, 0:512]
                    t_exp = kb.emit("act", (lambda e, s=MSKT[:, 0:span], o=PB[:, 0:span]:
                                            e.activation(o, s, AF.Exp, bias=BIA[:, 0:1],
                                                         scale=QK_SCALE,
                                                         accum_out=DEN[:, 0:1])),
                                    [t_b] + ([exp_war] if exp_war else []))
                    t_rc = kb.emit("dve", (lambda e: e.reciprocal(DEN[:, 1:2],
                                                                  DEN[:, 0:1])), [t_exp])
                    t_nm = kb.emit("dve", (lambda e, o=PB[:, 0:span]:
                                           e.tensor_scalar_mul(o, o, DEN[:, 1:2])), [t_rc])
                    if g == 0 and jp == 0 and hh == 0 and qi == 3 and "p0" in dbg:
                        kb.dma("sp", dbg["p0"], PB[:, 0:384], deps=[t_nm])
                    nb = span // 128
                    for b in range(nb):
                        j = kb0 // 128 + b
                        src = PB[:, b * 128:(b + 1) * 128]
                        dtr = [t_nm, t_id] + ([th.war] if th.war else []) + \
                            ([pt_war] if pt_war else [])
                        ttr = kb.emit("pe", (lambda e, s=src: e.transpose(
                            th.stage[0:128, 0:128], s, IDT)), dtr)
                        slot = j * 384 + (qi - j) * 128
                        th.war = kb.emit("dve", (lambda e, o=PTh(hh)[:, slot:slot + 128]:
                                                 e.tensor_copy(o, th.stage[0:128, 0:128])),
                                         [ttr])
                        pt_evacs.append(th.war)
                    exp_war = pt_evacs[-1]
            if g == 0 and jp == 0 and "pt0" in dbg:
                kb.dma("sp", dbg["pt0"], PTh(0)[:, 0:2048], deps=pt_evacs)
            last_h = {}
            for hh in range(2):
                bank = ps[:, (5 + hh) * 512:(6 + hh) * 512]
                pvdeps = pt_evacs + qkv_done["v"] + \
                    ([pv_war[hh]] if pv_war[hh] else [])
                mms = []
                for qc in range(4):
                    qi = g * 4 + qc
                    for j in range(max(0, qi - 2), qi + 1):
                        lhsT = VT(j)[:, (2 * jp + hh) * 64:(2 * jp + hh) * 64 + 64]
                        slot = j * 384 + (qi - j) * 128
                        rhs = PTh(hh)[:, slot:slot + 128]
                        o = bank[0:64, qc * 128:(qc + 1) * 128]
                        mms.append((o, lhsT, rhs))
                last_mm = None
                for mi, (o, l, r) in enumerate(mms):
                    last_mm = kb.emit("pe", (lambda e, o=o, l=l, r=r,
                                             st=(mi == 0), sp=(mi == len(mms) - 1):
                                             e.matmul(o, l, r, start=st, stop=sp)),
                                      pvdeps if mi == 0 else [],
                                      sem=("pe" if mi == len(mms) - 1 else False))
                last_h[hh] = last_mm
            pt_war = last_h[1]
            for hh in range(2):
                bank = ps[:, (5 + hh) * 512:(6 + hh) * 512]
                outap = ATG(jp)[0:64, :] if hh == 0 else ATG(jp)[64:128, :]
                tev = kb.emit("act", (lambda e, o=outap, b=bank[0:64, :]:
                                      e.activation(o, b, AF.Identity)),
                              [last_h[hh]] + ([pv_war[hh]] if pv_war[hh] else []))
                pv_war[hh] = tev
                at_done.append(tev)
            if g == 0 and jp == 0 and "atg0" in dbg:
                kb.dma("sp", dbg["atg0"], ATG(0), deps=[tev])
        for tl in range(4):
            ti = g * 4 + tl
            t_x2 = kb.dma("sp", XT2, io["x"][ti * 128:(ti + 1) * 128, :],
                          deps=[wo_war if wo_war else prev_ln])
            t_mix = []
            for eh in range(2):
                bank = ps[:, 7 * 512:8 * 512]
                dep0 = at_done + t_wo + ([wo_war] if wo_war else [])
                last_mm = None
                for dd in range(DC):
                    lhsT = ATG(dd)[:, tl * 128:(tl + 1) * 128]
                    rhs = WOC(dd)[:, eh * 512:(eh + 1) * 512]
                    last_mm = kb.emit("pe", (lambda e, o=bank, l=lhsT, r=rhs,
                                             st=(dd == 0), sp=(dd == DC - 1):
                                             e.matmul(o, l, r, start=st, stop=sp)),
                                      dep0 if dd == 0 else [],
                                      sem=("pe" if dd == DC - 1 else False))
                tm = kb.emit("dve", (lambda e, o=XNR[:, eh * 512:(eh + 1) * 512],
                                     b=bank, x=XT2[:, eh * 512:(eh + 1) * 512]:
                                     e.tensor_add(o, b, x)), [last_mm, t_x2])
                wo_war = tm
                t_mix.append(tm)
            t_out = kb.dma("sp", io["x_out"][ti * 128:(ti + 1) * 128, :], XNR,
                           deps=t_mix)
            outs.append(t_out)
            wo_war = t_out
    kb.emit("sp", lambda e: e.nop(), deps=outs, sem=False)
    return nc



def build_l0f_ffn(nc, kb, io):
    a = nc.alloc_sbuf_tensor("Abf", [128, 84128], BF16).ap()
    f = nc.alloc_sbuf_tensor("Af32", [128, 5248], F32).ap()
    ps = nc.alloc_psum_tensor("PS", [128, 4096], F32).ap()
    ROWS = nc.alloc_sbuf_tensor("ROWS", [1, 6912], BF16).ap()
    IDT = nc.alloc_sbuf_tensor("IDT", [128, 128], BF16).ap()

    UFM = lambda dd: a[:, dd * 2048:(dd + 1) * 2048]
    W1R = lambda dd: a[:, 16384 + dd * 5460:16384 + (dd + 1) * 5460]
    W2R = lambda i_c: a[:, 60064 + i_c * 1024:60064 + (i_c + 1) * 1024]
    MIDC = a[:, 82592:82848]
    GELS = a[:, 82848:83104]
    XNT = a[:, 83104:84128]

    B1R = ROWS[0:1, 0:5460]
    B2R = ROWS[0:1, 5504:6528]
    ONE1 = ROWS[0:1, 6528:6784]

    XT1 = f[:, 0:1024]
    SQ = f[:, 1024:2048]
    ST8 = f[:, 2048:2056]
    GC = f[:, 2056:2064]
    BC = f[:, 2064:2072]
    EPS = f[:, 2072:2073]
    XNR = f[:, 2080:3104]
    OACC = f[:, 3104:5152]
    kb.epscol = EPS

    CW = [128] * 21 + [42]
    COFF = [i * 256 for i in range(22)]

    t_eps = kb.emit("dve", lambda e: e.memset(EPS, 1e-5))
    t_one = kb.emit("dve", lambda e: e.memset(ONE1, 1.0))
    t_id = kb.dma("sp", IDT, io["ident"])
    t_gc = kb.dma("sp", GC, io["g_cols"])
    t_bc = kb.dma("sp", BC, io["b_cols"])
    t_b1 = kb.dma("pool", B1R, io["b1r"])
    t_b2 = kb.dma("pool", B2R, io["b2row"])
    t_w1 = [kb.dma("pool", W1R(dd), io["w1r"][dd * 128:(dd + 1) * 128, :])
            for dd in range(DC)]
    t_w2 = []
    for i_c in range(22):
        cw = CW[i_c]
        t_w2.append(kb.dma("pool", W2R(i_c)[0:cw, :],
                           io["w2r"][i_c * 128:i_c * 128 + cw, :]))

    th = TransposeHelper(kb, IDT, ps, bank_f32_col=4 * 512)

    t_u_done = []
    prev_ln = None
    for i in range(NT):
        xt = XT1
        d0 = ([prev_ln] if prev_ln else [])
        t_lo = kb.dma("sp", xt[:, 0:512], io["x"][i * 128:(i + 1) * 128, 0:512], deps=d0)
        if i == 0:
            t_h1 = kb.dma("sp", xt[1:128, 512:1024], io["x"][0:127, 512:1024], deps=d0)
            t_z = kb.emit("dve", (lambda e, x=xt: e.memset(x[0:1, 512:1024], 0.0)), d0)
            t_hi = [t_h1, t_z]
        else:
            t_hi = [kb.dma("sp", xt[:, 512:1024],
                           io["x"][i * 128 - 1:(i + 1) * 128 - 1, 512:1024], deps=d0)]
        t_st = emit_ln_stats(kb, xt, SQ, ST8, [t_lo, t_eps] + t_hi)
        t_ap = emit_ln_apply(kb, XNT, xt, ST8, [t_st] + ([th.war] if th.war else []))
        prev_ln = t_ap
        for b in range(8):
            src_blk = XNT[:, b * 128:(b + 1) * 128]
            dtr = [t_ap, t_id] + ([th.war] if th.war else [])
            ttr = kb.emit("pe", (lambda e, s=src_blk: e.transpose(
                th.stage[0:128, 0:128], s, IDT)), dtr)
            th.war = kb.emit("dve", (lambda e, o=UFM(b)[:, i * 128:(i + 1) * 128],
                                     gcc=GC[:, b:b + 1], bcc=BC[:, b:b + 1]:
                                     e.tensor_scalar(o, th.stage[0:128, 0:128],
                                                     gcc, bcc, ALU.mult, ALU.add)),
                             [ttr, t_gc, t_bc])
        t_u_done.append(th.war)

    outs = []
    hg_war = None
    prev_out_mm = None
    bank_war = {2: None, 3: None, 5: None, 6: None}
    for tq in range(8):
        for gi, (i0, i1) in enumerate(((0, 11), (11, 22))):
            oacc_done = {}
            for il, i_c in enumerate(range(i0, i1)):
                cw = CW[i_c]
                bka = ps[:, 0:256]
                bkg = ps[:, 512:768]
                dep0 = t_u_done + t_w1 + [t_b1, t_one] + \
                    ([hg_war] if hg_war else [])
                for dd in range(DC):
                    lha = W1R(dd)[:, COFF[i_c]:COFF[i_c] + cw]
                    rhs = UFM(dd)[:, tq * 256:(tq + 1) * 256]
                    kb.emit("pe", (lambda e, o=bka[0:cw, :], l=lha, r=rhs,
                                   st=(dd == 0): e.matmul(o, l, r, start=st, stop=False)),
                            dep0 if dd == 0 else [], sem=False)
                lma = kb.emit("pe", (lambda e, o=bka[0:cw, :],
                                     l=B1R[0:1, COFF[i_c]:COFF[i_c] + cw]:
                                     e.matmul(o, l, ONE1[0:1, 0:256], start=False, stop=True)),
                              [t_b1, t_one])
                for dd in range(DC):
                    lhg = W1R(dd)[:, COFF[i_c] + cw:COFF[i_c] + 2 * cw]
                    rhs = UFM(dd)[:, tq * 256:(tq + 1) * 256]
                    kb.emit("pe", (lambda e, o=bkg[0:cw, :], l=lhg, r=rhs,
                                   st=(dd == 0): e.matmul(o, l, r, start=st, stop=False)),
                            dep0 if dd == 0 else [], sem=False)
                lmg = kb.emit("pe", (lambda e, o=bkg[0:cw, :],
                                     l=B1R[0:1, COFF[i_c] + cw:COFF[i_c] + 2 * cw]:
                                     e.matmul(o, l, ONE1[0:1, 0:256], start=False, stop=True)),
                              [t_b1, t_one])
                t_gel = kb.emit("act", (lambda e, o=GELS[0:cw, :], b=bkg[0:cw, :]:
                                        e.activation(o, b, AF.Gelu)),
                                [lmg] + ([hg_war] if hg_war else []))
                t_mid = kb.emit("dve", (lambda e, o=MIDC[0:cw, :], b2=bka[0:cw, :],
                                        g=GELS[0:cw, :]: e.tensor_mul(o, b2, g)),
                                [t_gel, lma] +
                                ([prev_out_mm] if prev_out_mm else []))
                hg_war = t_mid
                for tl in range(2):
                    for oh in range(2):
                        bk = (2, 3, 5, 6)[tl * 2 + oh]
                        bank = ps[:, bk * 512:(bk + 1) * 512]
                        lhsT = MIDC[0:cw, tl * 128:(tl + 1) * 128]
                        rhs = W2R(i_c)[0:cw, oh * 512:(oh + 1) * 512]
                        dep3 = [t_mid, t_w2[i_c]]
                        if il == 0:
                            dep3 += [bank_war[bk]] if bank_war[bk] else []
                        last = (il == i1 - i0 - 1) and gi == 1
                        tmm = kb.emit("pe", (lambda e, o=bank, l=lhsT, r=rhs,
                                             st=(il == 0), sp=(il == i1 - i0 - 1):
                                             e.matmul(o, l, r, start=st, stop=sp)),
                                      dep3,
                                      sem=("pe" if il == i1 - i0 - 1 else False))
                        if il == i1 - i0 - 1:
                            oacc_done[(tl, oh)] = tmm
                        prev_out_mm = tmm if il == i1 - i0 - 1 else prev_out_mm
            if gi == 0:
                for tl in range(2):
                    for oh in range(2):
                        bk = (2, 3, 5, 6)[tl * 2 + oh]
                        bank = ps[:, bk * 512:(bk + 1) * 512]
                        o = OACC[:, tl * 1024 + oh * 512:tl * 1024 + (oh + 1) * 512]
                        t_sp = kb.emit("dve", (lambda e, o=o, b=bank:
                                               e.tensor_copy(o, b)),
                                       [oacc_done[(tl, oh)]])
                        bank_war[bk] = t_sp
            else:
                for tl in range(2):
                    ti = tq * 2 + tl
                    t_x2 = kb.dma("sp", XT1, io["x"][ti * 128:(ti + 1) * 128, :],
                                  deps=[outs[-1]] if outs else [prev_ln])
                    t_mix = []
                    for oh in range(2):
                        bk = (2, 3, 5, 6)[tl * 2 + oh]
                        bank = ps[:, bk * 512:(bk + 1) * 512]
                        tm1 = kb.emit("dve", (lambda e, o=XNR[:, oh * 512:(oh + 1) * 512],
                                              b=bank,
                                              oa=OACC[:, tl * 1024 + oh * 512:tl * 1024 + (oh + 1) * 512]:
                                              e.tensor_add(o, b, oa)),
                                      [oacc_done[(tl, oh)]] + ([outs[-1]] if outs else []))
                        bank_war[bk] = tm1
                        tm2 = kb.emit("dve", (lambda e, o=XNR[:, oh * 512:(oh + 1) * 512],
                                              x=XT1[:, oh * 512:(oh + 1) * 512]:
                                              e.tensor_add(o, o, x)), [tm1, t_x2])
                        bk7 = ps[:, 7 * 512:8 * 512]
                        tb2 = kb.emit("pe", (lambda e, r=B2R[0:1, oh * 512:(oh + 1) * 512]:
                                             e.matmul(bk7, ONE1[0:1, 0:128], r,
                                                      start=True, stop=True)),
                                      [t_b2, t_one] +
                                      ([bank_war.get(7)] if bank_war.get(7) else []))
                        tm3 = kb.emit("dve", (lambda e, o=XNR[:, oh * 512:(oh + 1) * 512]:
                                              e.tensor_add(o, o, bk7)), [tb2, tm2])
                        bank_war[7] = tm3
                        t_mix.append(tm3)
                    t_out = kb.dma("sp", io["x_out"][ti * 128:(ti + 1) * 128, :], XNR,
                                   deps=t_mix)
                    outs.append(t_out)
    kb.emit("sp", lambda e: e.nop(), deps=outs, sem=False)
    return nc


def host_prep_l0f(inputs):
    rt = host_rotary_tables()
    msks = host_l0f_masks()
    Wqkv = np.asarray(inputs["L0_ffn_Wqkv"])
    Wo = np.asarray(inputs["L0_ffn_Wo"])
    W1 = np.asarray(inputs["L0_ffn_W1"])
    b1 = np.asarray(inputs["L0_ffn_b1"])
    W2 = np.asarray(inputs["L0_ffn_W2"])
    b2 = np.asarray(inputs["L0_ffn_b2"])
    qsc = np.asarray(inputs["L0_ffn_q_scale"])
    ksc = np.asarray(inputs["L0_ffn_k_scale"])
    g = np.asarray(inputs["L0_ffn_ln_g"])
    bb = np.asarray(inputs["L0_ffn_ln_b"])
    CW = [128] * 21 + [42]

    def reorder_w1(w):
        cols = []
        for i in range(22):
            c0 = i * 128
            cols.append(w[:, c0:c0 + CW[i]])
            cols.append(w[:, 2730 + c0:2730 + c0 + CW[i]])
        return np.concatenate(cols, axis=1)

    def reorder_b1(b):
        cols = []
        for i in range(22):
            c0 = i * 128
            cols.append(b[c0:c0 + CW[i]])
            cols.append(b[2730 + c0:2730 + c0 + CW[i]])
        return np.concatenate(cols)[None, :]

    shared = {
        "ident": host_identity128(),
        "cs_q_fm": np.ascontiguousarray(rt["cs_q"].T).astype(BF),
        "sn_q_fm": np.ascontiguousarray(rt["sn_q"].T).astype(BF),
        "cs_k_fm": np.ascontiguousarray(rt["cs_k"].T).astype(BF),
        "sn_k_fm": np.ascontiguousarray(rt["sn_k"].T).astype(BF),
        "masks": msks,
    }
    per_layer = []
    for l in range(6):
        per_layer.append({
            "attn": dict(shared,
                         wqkv=np.ascontiguousarray(Wqkv[l]),
                         wo=np.ascontiguousarray(Wo[l]),
                         qs_cols=np.ascontiguousarray(
                             np.stack([qsc[l], ksc[l]], axis=1))),
            "ffn": {
                "ident": shared["ident"],
                "w1r": np.ascontiguousarray(reorder_w1(W1[l])),
                "b1r": np.ascontiguousarray(reorder_b1(b1[l])),
                "w2r": np.ascontiguousarray(W2[l]),
                "b2row": np.ascontiguousarray(b2[l][None, :]),
                "g_cols": np.ascontiguousarray(g[l].reshape(8, 128).T),
                "b_cols": np.ascontiguousarray(bb[l].reshape(8, 128).T),
            },
        })
    return per_layer


def make_l0f_attn_nc():
    nc = bass.Bass("TRN2", target_bir_lowering=False)
    io = {
        "x": nc.dram_tensor("x", [T, D], F32, kind="ExternalInput").ap(),
        "wqkv": nc.dram_tensor("wqkv", [1024, 3072], F32, kind="ExternalInput").ap(),
        "wo": nc.dram_tensor("wo", [1024, 1024], F32, kind="ExternalInput").ap(),
        "qs_cols": nc.dram_tensor("qs_cols", [64, 2], F32, kind="ExternalInput").ap(),
        "cs_q_fm": nc.dram_tensor("cs_q_fm", [64, 2048], BF16, kind="ExternalInput").ap(),
        "sn_q_fm": nc.dram_tensor("sn_q_fm", [64, 2048], BF16, kind="ExternalInput").ap(),
        "cs_k_fm": nc.dram_tensor("cs_k_fm", [64, 2048], BF16, kind="ExternalInput").ap(),
        "sn_k_fm": nc.dram_tensor("sn_k_fm", [64, 2048], BF16, kind="ExternalInput").ap(),
        "masks": nc.dram_tensor("masks", [3, 128, 384], F32, kind="ExternalInput").ap(),
        "ident": nc.dram_tensor("ident", [128, 128], BF16, kind="ExternalInput").ap(),
        "x_out": nc.dram_tensor("x_out", [T, D], F32, kind="ExternalOutput").ap(),
    }
    kb = KB()
    build_l0f_attn(nc, kb, io)
    kb.finalize(nc)
    return nc


def make_l0f_ffn_nc():
    nc = bass.Bass("TRN2", target_bir_lowering=False)
    io = {
        "x": nc.dram_tensor("x", [T, D], F32, kind="ExternalInput").ap(),
        "w1r": nc.dram_tensor("w1r", [1024, 5460], F32, kind="ExternalInput").ap(),
        "b1r": nc.dram_tensor("b1r", [1, 5460], F32, kind="ExternalInput").ap(),
        "w2r": nc.dram_tensor("w2r", [2730, 1024], F32, kind="ExternalInput").ap(),
        "b2row": nc.dram_tensor("b2row", [1, 1024], F32, kind="ExternalInput").ap(),
        "g_cols": nc.dram_tensor("g_cols", [128, 8], F32, kind="ExternalInput").ap(),
        "b_cols": nc.dram_tensor("b_cols", [128, 8], F32, kind="ExternalInput").ap(),
        "ident": nc.dram_tensor("ident", [128, 128], BF16, kind="ExternalInput").ap(),
        "x_out": nc.dram_tensor("x_out", [T, D], F32, kind="ExternalOutput").ap(),
    }
    kb = KB()
    build_l0f_ffn(nc, kb, io)
    kb.finalize(nc)
    return nc



def build_lnpre(nc, kb, io):
    f = nc.alloc_sbuf_tensor("Af32", [128, 4096], F32).ap()
    XT = f[:, 0:1024]
    SQ = f[:, 1024:2048]
    ST8 = f[:, 2048:2056]
    EPS = f[:, 2056:2057]
    XO = f[:, 2080:3104]
    kb.epscol = EPS
    t_eps = kb.emit("dve", lambda e: e.memset(EPS, 1e-5))
    prev = None
    outs = []
    for i in range(NT):
        t_x = kb.dma("sp", XT, io["x"][i * 128:(i + 1) * 128, :],
                     deps=[prev] if prev else [])
        t_st = emit_ln_stats(kb, XT, SQ, ST8, [t_x, t_eps])
        t_ap = emit_ln_apply(kb, XO, XT, ST8, [t_st] + ([prev] if prev else []))
        t_o = kb.dma("sp", io["x_out"][i * 128:(i + 1) * 128, :], XO, deps=[t_ap])
        outs.append(t_o)
        prev = t_o
    kb.emit("sp", lambda e: e.nop(), deps=outs, sem=False)
    return nc


def build_lnpost(nc, kb, io):
    f = nc.alloc_sbuf_tensor("Af32", [128, 4608], F32).ap()
    XT = f[:, 0:1024]
    YT = f[:, 1024:2048]
    SQ = f[:, 2048:3072]
    ST8 = f[:, 3072:3080]
    EPS = f[:, 3080:3081]
    XO = f[:, 3088:4112]
    kb.epscol = EPS
    t_eps = kb.emit("dve", lambda e: e.memset(EPS, 1e-5))
    prev = None
    outs = []
    for i in range(NT):
        d0 = [prev] if prev else []
        t_x = kb.dma("sp", XT, io["x"][i * 128:(i + 1) * 128, :], deps=d0)
        t_y = kb.dma("sp", YT, io["y"][i * 128:(i + 1) * 128, :], deps=d0)
        t_st = emit_ln_stats(kb, YT, SQ, ST8, [t_y, t_eps])
        t_ap = emit_ln_apply(kb, XO, YT, ST8, [t_st] + d0)
        t_ad = kb.emit("dve", (lambda e: e.tensor_add(XO, XO, XT)), [t_ap, t_x])
        t_o = kb.dma("sp", io["x_out"][i * 128:(i + 1) * 128, :], XO, deps=[t_ad])
        outs.append(t_o)
        prev = t_o
    kb.emit("sp", lambda e: e.nop(), deps=outs, sem=False)
    return nc


def make_lnpre_nc():
    nc = bass.Bass("TRN2", target_bir_lowering=False)
    io = {
        "x": nc.dram_tensor("x", [T, D], F32, kind="ExternalInput").ap(),
        "x_out": nc.dram_tensor("x_out", [T, D], F32, kind="ExternalOutput").ap(),
    }
    kb = KB()
    build_lnpre(nc, kb, io)
    kb.finalize(nc)
    return nc


def make_lnpost_nc():
    nc = bass.Bass("TRN2", target_bir_lowering=False)
    io = {
        "x": nc.dram_tensor("x", [T, D], F32, kind="ExternalInput").ap(),
        "y": nc.dram_tensor("y", [T, D], F32, kind="ExternalInput").ap(),
        "x_out": nc.dram_tensor("x_out", [T, D], F32, kind="ExternalOutput").ap(),
    }
    kb = KB()
    build_lnpost(nc, kb, io)
    kb.finalize(nc)
    return nc



_NC_CACHE = {}


def _get_nc(name, maker):
    if name not in _NC_CACHE:
        _NC_CACHE[name] = maker()
    return _NC_CACHE[name]


def _run(nc, per_core_maps):
    res = run_bass_kernel_spmd(nc, per_core_maps, core_ids=list(range(N_CORES)))
    return np.stack([r["x_out"] for r in res.results])


def kernel(**inputs):
    x = np.ascontiguousarray(np.asarray(inputs["L0_attn_x"], dtype=np.float32))

    nc = _get_nc("l0attn", make_l0attn_nc)
    hp = host_prep_l0attn(inputs)
    x = _run(nc, [dict(hp, x=np.ascontiguousarray(x[b])) for b in range(N_CORES)])

    ncpre = _get_nc("lnpre", make_lnpre_nc)
    ncpost = _get_nc("lnpost", make_lnpost_nc)
    y = _run(ncpre, [{"x": np.ascontiguousarray(x[b])} for b in range(N_CORES)])
    pl = host_prep_l0f(inputs)
    nca = _get_nc("l0f_attn", make_l0f_attn_nc)
    ncf = _get_nc("l0f_ffn", make_l0f_ffn_nc)
    for l in range(6):
        y = _run(nca, [dict(pl[l]["attn"], x=np.ascontiguousarray(y[b]))
                       for b in range(N_CORES)])
        y = _run(ncf, [dict(pl[l]["ffn"], x=np.ascontiguousarray(y[b]))
                       for b in range(N_CORES)])
    x = _run(ncpost, [{"x": np.ascontiguousarray(x[b]),
                       "y": np.ascontiguousarray(y[b])} for b in range(N_CORES)])

    nc = _get_nc("l1attn", make_l1attn_nc)
    hp = host_prep_l1attn(inputs)
    x = _run(nc, [dict(hp, x=np.ascontiguousarray(x[b])) for b in range(N_CORES)])

    nc = _get_nc("l1ffn", make_l1ffn_nc)
    hp = host_prep_l1ffn(inputs)
    x = _run(nc, [dict(hp, x=np.ascontiguousarray(x[b])) for b in range(N_CORES)])

    return x.astype(np.float32)
